# version 15 (final)
# speedup vs baseline: 1.1777x; 1.0055x over previous
import numpy as np
import ml_dtypes
import concourse.bass as bass
import concourse.mybir as mybir
from concourse.bass import broadcast_tensor_aps
from concourse.bass_utils import run_bass_kernel_spmd
from contextlib import ExitStack

F32 = mybir.dt.float32
BF16 = mybir.dt.bfloat16
I32 = mybir.dt.int32
AF = mybir.ActivationFunctionType
ALU = mybir.AluOpType
AX = mybir.AxisListType

B, S, D, NH, NKV, DH, IFF = 4, 4096, 2048, 16, 4, 128, 5632
TOWN = 2048
NDT = D // 128
NIT = IFF // 128
P_CH = [[0, 3, 4, 7], [1, 2, 5, 6]]
SLOT_KV = [8, 16, 24, 32]
RB = 256.5
SCALE = float(np.float32(1.0) / np.float32(np.sqrt(np.float32(DH))))

ENGS = ("sync", "tensor", "vector", "scalar", "gpsimd")


class Sem:
    def __init__(self, h):
        self.h = h
        self.n = 0


class Prog:
    def __init__(self):
        self.ops = {e: [] for e in ENGS}

    def emit(self, eng, fn, waits=(), inc=None):
        ws = tuple((s, v) for (s, v) in waits if v > 0)
        self.ops[eng].append((ws, fn, inc))
        if inc is not None:
            inc[0].n += inc[1]


def build_program():
    nc = bass.Bass("TRN2", target_bir_lowering=False, debug=False)

    def din(name, shape, dt=F32):
        return nc.dram_tensor(name, shape, dt, kind="ExternalInput").ap()

    x_kv = din("x_kv", [S, D])
    x_q = din("x_q", [TOWN, D])
    wq_t = din("wq_t", [D, D])
    wk_t = din("wk_t", [D, NKV * DH])
    wv_t = din("wv_t", [D, NKV * DH])
    wo_t = din("wo_t", [D, D])
    gw_t = din("gw_t", [D, IFF])
    uw_t = din("uw_t", [D, IFF])
    dw_t = din("dw_t", [IFF, D])
    gs_in = din("gs", [128, NIT])
    us_in = din("us", [128, NIT])
    ds_in = din("ds", [128, NDT])
    masks = din("masks", [32, 128, 512], BF16)
    ccq = din("ccq", [128, TOWN]); ssq = din("ssq", [128, TOWN])
    cck = din("cck", [128, S]); ssk = din("ssk", [128, S])
    ident_bf = din("ident_bf", [128, 128], BF16)
    ident_f = din("ident_f", [128, 128], F32)
    onehot = din("onehot", [128, NH * 16], BF16)
    consts = din("consts", [128, 132], F32)
    out = nc.dram_tensor("out", [TOWN, D], F32, kind="ExternalOutput").ap()

    q_scr = nc.dram_tensor("q_scr", [NH, 128, TOWN], BF16, kind="Internal").ap()
    ffn_scr = nc.dram_tensor("ffn_scr", [TOWN, D], F32, kind="Internal").ap()

    prog = Prog()
    root = ExitStack()

    def sem(name):
        return Sem(root.enter_context(nc.semaphore(name)))

    pe = sem("pe"); ve = sem("ve"); ac = sem("ac")
    s_const = sem("s_const")
    s_x = [sem("s_x0"), sem("s_x1")]
    s_xb = [sem("s_xb0"), sem("s_xb1"), sem("s_xb2")]
    s_rope = [sem("s_rope0"), sem("s_rope1")]
    s_mask = sem("s_mask")
    s_wb = [sem("s_wb0"), sem("s_wb1")]
    s_gw = [sem("s_gw0"), sem("s_gw1"), sem("s_gw2")]
    s_uw = [sem("s_uw0"), sem("s_uw1"), sem("s_uw2")]
    s_dw = [sem("s_dw0"), sem("s_dw1"), sem("s_dw2")]
    s_fst = [sem("s_fst0"), sem("s_fst1")]
    s_qst = [sem("s_qst0"), sem("s_qst1")]
    s_qh = [sem("s_qh0"), sem("s_qh1")]
    s_xr = [sem("s_xr0"), sem("s_xr1")]
    s_fr = [sem("s_fr0"), sem("s_fr1")]
    s_ost = [[sem(f"s_ost{i}_{j}") for j in range(4)] for i in range(2)]

    def sb(name, shape, dt, stack=None, side=None):
        return (stack or root).enter_context(nc.sbuf_tensor(name, shape, dt, side=side))

    def ps(name, shape, dt, stack=None):
        return (stack or root).enter_context(nc.psum_tensor(name, shape, dt))

    E = prog.emit

    c_identbf = sb("c_identbf", [128, 128], BF16)
    c_identf = sb("c_identf", [128, 128], F32)
    c_onehot = sb("c_onehot", [128, NH * 16], BF16)
    c_consts = sb("c_consts", [128, 132], F32)
    c_gs = sb("c_gs", [128, NIT], F32)
    c_us = sb("c_us", [128, NIT], F32)
    c_ds = sb("c_ds", [128, NDT], F32)
    for dst, src in [(c_identbf, ident_bf), (c_identf, ident_f), (c_onehot, onehot),
                     (c_consts, consts), (c_gs, gs_in), (c_us, us_in), (c_ds, ds_in)]:
        E("sync", lambda e, d=dst, s=src: e.dma_start(out=d[:], in_=s), inc=(s_const, 16))
    bC1 = c_consts[:, 0:1]
    bmC1 = c_consts[:, 3:4]
    onesf = c_consts[:, 1:2]
    onesrow = c_consts[0:1, 4:132]
    epsb = c_consts[:, 2:3]
    CONST_DONE = s_const.n
    c_junk = sb("c_junk", [128, D], F32)

    def ln_stats(stack, tag, junk, nt=4):
        return dict(
            sm=sb(f"{tag}_sm", [128, nt], F32, stack), ssq=sb(f"{tag}_ssq", [128, nt], F32, stack),
            m=sb(f"{tag}_m", [128, nt], F32, stack), msq=sb(f"{tag}_msq", [128, nt], F32, stack),
            var=sb(f"{tag}_var", [128, nt], F32, stack), lnv=sb(f"{tag}_lnv", [128, nt], F32, stack),
            rstd=sb(f"{tag}_rstd", [128, nt], F32, stack), nb=sb(f"{tag}_nb", [128, nt], F32, stack),
            junk=junk, nt=nt,
        )

    def emit_ln(xch, stats, wait_load, per_tile_fn=None):
        nt = stats["nt"]
        sm, ssq_, mmean, rstd, nbias = (stats[k] for k in ("sm", "ssq", "m", "rstd", "nb"))
        for t in range(nt):
            E("vector", lambda e, t=t: e.reduce_sum(sm[:, t:t + 1], xch[:, t, :], AX.X),
              waits=[wait_load, (ve, ve.n)], inc=(ve, 1))
            E("scalar", lambda e, t=t: e.activation(stats["junk"][:], xch[:, t, :],
                                                    AF.Square, accum_out=ssq_[:, t:t + 1]),
              waits=[wait_load, (ac, ac.n)], inc=(ac, 1))
        vw, aw = ve.n, ac.n
        E("vector", lambda e: e.tensor_scalar(mmean[:], sm[:], 1.0 / D, None, ALU.mult),
          waits=[(ve, vw), (ac, aw)], inc=(ve, 1))
        E("vector", lambda e: e.tensor_tensor(stats["msq"][:], mmean[:], mmean[:], ALU.mult),
          waits=[(ve, ve.n)], inc=(ve, 1))
        E("vector", lambda e: e.tensor_scalar(stats["var"][:], ssq_[:], 1.0 / D, None, ALU.mult),
          waits=[(ve, ve.n)], inc=(ve, 1))
        E("vector", lambda e: e.tensor_sub(stats["var"][:], stats["var"][:], stats["msq"][:]),
          waits=[(ve, ve.n)], inc=(ve, 1))
        E("scalar", lambda e: e.activation(stats["lnv"][:], stats["var"][:], AF.Ln, bias=epsb),
          waits=[(ve, ve.n), (ac, ac.n)], inc=(ac, 1))
        E("scalar", lambda e: e.activation(rstd[:], stats["lnv"][:], AF.Exp, scale=-0.5),
          waits=[(ac, ac.n)], inc=(ac, 1))
        E("vector", lambda e: e.tensor_tensor(nbias[:], mmean[:], rstd[:], ALU.mult),
          waits=[(ac, ac.n), (ve, ve.n)], inc=(ve, 1))
        E("vector", lambda e: e.tensor_scalar(nbias[:], nbias[:], -1.0, None, ALU.mult),
          waits=[(ve, ve.n)], inc=(ve, 1))
        stat_done = (ve, ve.n)
        if per_tile_fn is not None:
            for t in range(nt):
                per_tile_fn(t, stat_done)
        return stat_done

    def emit_quant_a(xn_ap, mxv, sxv, tcol, xn_ready_w, qst):
        E("vector", lambda e: e.tensor_reduce(mxv[:, tcol], xn_ap, AX.X, ALU.max,
                                              apply_absolute_value=True),
          waits=[xn_ready_w, (ve, ve.n)], inc=(ve, 1))
        E("vector", lambda e: e.tensor_scalar(mxv[:, tcol], mxv[:, tcol], 1e-5, 1.0 / 127.0,
                                              ALU.max, ALU.mult),
          waits=[(ve, ve.n)], inc=(ve, 1))
        E("vector", lambda e: e.reciprocal(sxv[:, tcol], mxv[:, tcol]),
          waits=[(ve, ve.n)], inc=(ve, 1))
        E("scalar", lambda e: e.activation(xn_ap, xn_ap, AF.Identity,
                                           scale=sxv[:, tcol], bias=bC1),
          waits=[(ve, ve.n), (ac, ac.n)], inc=(ac, 1))
        E("scalar", lambda e: e.activation(xi_t_of(qst, tcol), xn_ap, AF.Identity, bias=bmC1),
          waits=[(ac, ac.n), (ve, qst["xi_ve"][_tc(tcol) % qst["nxi"]])], inc=(ac, 1))
        return ac.n

    def _tc(tcol):
        return tcol.start

    def xi_t_of(qst, tcol):
        return qst["xi"][:, _tc(tcol) % qst["nxi"], :]

    def emit_quant_b(xq_ap, mxv, tcol, round_ac, qst):
        E("vector", lambda e: e.tensor_scalar(xq_ap[:], xi_t_of(qst, tcol), mxv[:, tcol], None,
                                              ALU.mult),
          waits=[(ac, round_ac), (pe, qst["tr_pe"][_tc(tcol) % 2])], inc=(ve, 1))
        qst["xi_ve"][_tc(tcol) % qst["nxi"]] = ve.n

    def emit_tr16(src, dst_fn, ptr, src_ready_w, tag_rot, ptr_ve=None, extra_w=()):
        for g in range(2):
            bi = (tag_rot + g) % 2
            bank = ptr[:, bi, :]
            war = (ve, ptr_ve[bi]) if ptr_ve is not None else (ve, ve.n)
            E("tensor", lambda e, g=g, bank=bank, src=src: [
                e.transpose(bank[:, j * 128:(j + 1) * 128],
                            src[:, (g * 8 + j) * 128:(g * 8 + j + 1) * 128],
                            c_identbf[:]) for j in range(8)][-1],
              waits=[src_ready_w, war] + list(extra_w), inc=(pe, 1))
            E("vector", lambda e, g=g, bank=bank: e.tensor_copy(
                dst_fn(g), bank.rearrange("p (a b) -> p a b", b=128)),
              waits=[(pe, pe.n)], inc=(ve, 1))
            if ptr_ve is not None:
                ptr_ve[bi] = ve.n

    s_pw = [sem("s_pw0"), sem("s_pw1")]
    pp_wbuf = [sb(f"pp_wb{i}", [128, NDT, 32], F32) for i in range(2)]
    pre_red = sb("pre_red", [128, 160], F32)
    prep_state = dict(rdv=[ve.n, ve.n], nb=0, done_ve=[ve.n])
    spare_ops = []

    def make_prepass(w_ap, ncols, col0):
        OC2 = 32
        nch = ncols // OC2
        for c in range(nch):
            def emit_one(c=c, col0=col0, w_ap=w_ap):
                bi = prep_state["nb"] % 2
                prep_state["nb"] += 1
                E("sync", lambda e, c=c, bi=bi, w_ap=w_ap: e.dma_start(
                    out=pp_wbuf[bi][:], in_=w_ap[:, c * 32:(c + 1) * 32]
                    .rearrange("(t p) f -> p t f", p=128)),
                  waits=[(ve, prep_state["rdv"][bi]), (s_const, CONST_DONE)],
                  inc=(s_pw[bi], 16))
                E("vector", lambda e, bi=bi, c=c, col0=col0: e.tensor_reduce(
                    pre_red[:, col0 + c:col0 + c + 1],
                    pp_wbuf[bi][:].rearrange("p a b -> p (a b)"), AX.X, ALU.add,
                    apply_absolute_value=True),
                  waits=[(s_pw[bi], s_pw[bi].n), (ve, ve.n)], inc=(ve, 1))
                prep_state["rdv"][bi] = ve.n
                prep_state["done_ve"][0] = ve.n
            spare_ops.append(emit_one)

    make_prepass(wk_t, NKV * DH, 0)
    make_prepass(wv_t, NKV * DH, 16)
    make_prepass(wq_t, D, 32)
    make_prepass(wo_t, D, 96)

    def spare_drain(k=1):
        for _ in range(k):
            if spare_ops:
                spare_ops.pop(0)()

    def phase_fence(store_waits=()):
        w = [(pe, pe.n), (ve, ve.n), (ac, ac.n)] + list(store_waits)
        for eng in ("tensor", "vector", "scalar", "sync", "gpsimd"):
            E(eng, None, waits=w)

    def phase_f():
        fstack = ExitStack()
        xnT = sb("xnT", [128, NDT, TOWN], BF16, fstack)
        with fstack:
            f1 = ExitStack()
            with f1:
                xch = sb("f_xch", [128, 2, 4, D], F32, f1)
                xn_t = sb("f_xn", [128, D], F32, f1)
                xnb_t = sb("f_xnb", [128, D], BF16, f1)
                stats2 = [ln_stats(f1, "fs0", c_junk), ln_stats(f1, "fs1", c_junk)]
                ptr = ps("f_ptr", [128, 2, 1024], BF16, f1)
                xch_done = [(ve.n, ac.n), (ve.n, ac.n)]
                last_tr = [pe.n]
                xn_ve_f = [ve.n]
                ptr_ve1 = [ve.n, ve.n]
                sdone = [None, None]

                def f1_load(c):
                    cb = c % 2
                    ldw = (s_x[cb], s_x[cb].n + 16)
                    E("sync", lambda e, c=c, cb=cb: e.dma_start(
                        out=xch[:, cb, :, :], in_=x_q[c * 512:(c + 1) * 512, :]
                        .rearrange("(t p) f -> p t f", p=128)),
                      waits=[(ve, xch_done[cb][0]), (ac, xch_done[cb][1]),
                             (s_const, CONST_DONE)], inc=(s_x[cb], 16))
                    return ldw

                def f1_tile(c, t):
                    cb = c % 2
                    stats = stats2[cb]
                    E("scalar", lambda e, t=t, cb=cb, stats=stats: e.activation(
                        xn_t[:], xch[:, cb, t, :], AF.Identity,
                        bias=stats["nb"][:, t:t + 1], scale=stats["rstd"][:, t:t + 1]),
                      waits=[sdone[cb], (ve, xn_ve_f[0]), (ac, ac.n)], inc=(ac, 1))
                    E("vector", lambda e: e.tensor_copy(xnb_t[:], xn_t[:]),
                      waits=[(ac, ac.n), (pe, last_tr[0])], inc=(ve, 1))
                    xn_ve_f[0] = ve.n
                    rdy = (ve, ve.n)
                    emit_tr16(xnb_t,
                              lambda g, c=c, t=t: xnT[:, g * 8:(g + 1) * 8,
                                                      c * 512 + t * 128: c * 512 + (t + 1) * 128],
                              ptr, rdy, t, ptr_ve1)
                    last_tr[0] = pe.n

                ldw0 = f1_load(0)
                sdone[0] = emit_ln(xch[:, 0, :, :], stats2[0], ldw0)
                for c in range(4):
                    cb = c % 2
                    if c + 1 < 4:
                        ldw1 = f1_load(c + 1)
                    f1_tile(c, 0)
                    f1_tile(c, 1)
                    if c + 1 < 4:
                        sdone[(c + 1) % 2] = emit_ln(xch[:, (c + 1) % 2, :, :],
                                                     stats2[(c + 1) % 2], ldw1)
                    f1_tile(c, 2)
                    f1_tile(c, 3)
                    xch_done[cb] = (ve.n, ac.n)
            HT = TOWN // 2
            hdnT = sb("hdnT", [128, NIT, HT], BF16, fstack)

            wpe = [0, 0, 0]; dpe = [0, 0, 0]; stv = [0, 0]

            def do_half(half):
                f2 = ExitStack()
                with f2:
                    gwb = [sb(f"f_gw{i}_h{half}", [128, NDT, 128], BF16, f2) for i in range(3)]
                    uwb = [sb(f"f_uw{i}_h{half}", [128, NDT, 128], BF16, f2) for i in range(3)]
                    sg = sb(f"f_sg_h{half}", [128, HT], BF16, f2)
                    su = sb(f"f_su_h{half}", [128, HT], BF16, f2)
                    pg = ps(f"f_pg_h{half}", [128, 2, HT], F32, f2)
                    pu = ps(f"f_pu_h{half}", [128, 2, HT], F32, f2)
                    fence = [(ve, ve.n), (ac, ac.n), (pe, pe.n),
                             (s_fst[0], s_fst[0].n), (s_fst[1], s_fst[1].n)]
                    silu_ac = [ac.n, ac.n]
                    iden_ac = [ac.n, ac.n]
                    for i in range(NIT):
                        bi = i % 2
                        wb = i % 3
                        E("gpsimd", lambda e, i=i, wb=wb: e.dma_start(
                            out=gwb[wb][:], in_=gw_t[:, i * 128:(i + 1) * 128]
                            .rearrange("(t p) f -> p t f", p=128)),
                          waits=[(pe, wpe[wb])] + (fence if i < 3 else []), inc=(s_gw[wb], 16))
                        E("gpsimd", lambda e, i=i, wb=wb: e.dma_start(
                            out=uwb[wb][:], in_=uw_t[:, i * 128:(i + 1) * 128]
                            .rearrange("(t p) f -> p t f", p=128)),
                          waits=[(pe, wpe[wb])] + (fence if i < 3 else []), inc=(s_uw[wb], 16))
                        wready = [(s_gw[wb], s_gw[wb].n), (s_uw[wb], s_uw[wb].n)]
                        E("tensor", lambda e, bi=bi, wb=wb, half=half: [
                            e.matmul(pg[:, bi, tc * 512:(tc + 1) * 512], gwb[wb][:, dt, :],
                                     xnT[:, dt, half * HT + tc * 512: half * HT + (tc + 1) * 512],
                                     start=(dt == 0), stop=(dt == NDT - 1))
                            for dt in range(NDT) for tc in range(2)][-1],
                          waits=wready + [(ac, silu_ac[bi])] + (fence if i < 2 else []), inc=(pe, 1))
                        E("tensor", lambda e, bi=bi, wb=wb, half=half: [
                            e.matmul(pu[:, bi, tc * 512:(tc + 1) * 512], uwb[wb][:, dt, :],
                                     xnT[:, dt, half * HT + tc * 512: half * HT + (tc + 1) * 512],
                                     start=(dt == 0), stop=(dt == NDT - 1))
                            for dt in range(NDT) for tc in range(2)][-1],
                          waits=[(ac, iden_ac[bi])], inc=(pe, 1))
                        wpe[wb] = pe.n
                        pemm = pe.n
                        E("scalar", lambda e, i=i, bi=bi: e.activation(sg[:], pg[:, bi, :], AF.Silu,
                                                                       scale=c_gs[:, i:i + 1]),
                          waits=[(pe, pemm - 1), (ve, ve.n), (ac, ac.n)], inc=(ac, 1))
                        silu_ac[bi] = ac.n
                        E("scalar", lambda e, i=i, bi=bi: e.activation(su[:], pu[:, bi, :], AF.Identity,
                                                                       scale=c_us[:, i:i + 1]),
                          waits=[(pe, pemm), (ve, ve.n), (ac, ac.n)], inc=(ac, 1))
                        iden_ac[bi] = ac.n
                        E("vector", lambda e, i=i: e.tensor_mul(hdnT[:, i, :], sg[:], su[:]),
                          waits=[(ac, ac.n)], inc=(ve, 1))
                        spare_drain(2)
                f3 = ExitStack()
                with f3:
                    dwb = [sb(f"f_dw{i}_h{half}", [128, NIT, 128], BF16, f3) for i in range(2)]
                    fft = sb(f"f_fft_h{half}", [128, HT], F32, f3)
                    ffstage = [sb(f"f_ffs{i}_h{half}", [128, HT // 128, 128], F32, f3) for i in range(2)]
                    pd = ps(f"f_pd_h{half}", [128, 2, HT], F32, f3)
                    ptf = ps(f"f_ptf_h{half}", [128, 2, 512], F32, f3)
                    fence3 = [(ve, ve.n), (ac, ac.n), (pe, pe.n),
                              (s_fst[0], s_fst[0].n), (s_fst[1], s_fst[1].n)]
                    HDOK = (ve, ve.n)
                    fft_ac = [ac.n, ac.n]
                    ptf_ve = [ve.n, ve.n]
                    f3pend = [None]
                    for h in range(NDT):
                        pb = h % 2
                        wb = h % 2
                        E("gpsimd", lambda e, h=h, wb=wb: e.dma_start(
                            out=dwb[wb][:], in_=dw_t[:, h * 128:(h + 1) * 128]
                            .rearrange("(t p) f -> p t f", p=128)),
                          waits=[(pe, dpe[wb])] + (fence3 if h < 2 else []), inc=(s_dw[wb], 16))
                        wready = [(s_dw[wb], s_dw[wb].n)]
                        E("tensor", lambda e, pb=pb, wb=wb: [
                            e.matmul(pd[:, pb, tc * 512:(tc + 1) * 512], dwb[wb][:, it, :],
                                     hdnT[:, it, tc * 512:(tc + 1) * 512],
                                     start=(it == 0), stop=(it == NIT - 1))
                            for it in range(NIT) for tc in range(2)][-1],
                          waits=wready + [(ac, fft_ac[pb]), HDOK], inc=(pe, 1))
                        dpe[wb] = pe.n
                        if f3pend[0] is not None:
                            f3pend[0]()
                            f3pend[0] = None
                        E("scalar", lambda e, h=h, pb=pb: e.activation(fft[:], pd[:, pb, :], AF.Identity,
                                                                       scale=c_ds[:, h:h + 1]),
                          waits=[(pe, pe.n)], inc=(ac, 1))
                        fft_ac[pb] = ac.n
                        FOK = (ac, ac.n)

                        def f3tail(h=h, pb=pb, half=half, FOK=FOK):
                            for g in range(2):
                                E("tensor", lambda e, g=g: [
                                    e.transpose(ptf[:, g % 2, j * 128:(j + 1) * 128],
                                                fft[:, (g * 4 + j) * 128:(g * 4 + j + 1) * 128],
                                                c_identf[:]) for j in range(4)][-1],
                                  waits=[FOK, (ve, ptf_ve[g]), (s_fst[pb], stv[pb])], inc=(pe, 1))
                                E("vector", lambda e, g=g, pb=pb: e.tensor_copy(
                                    ffstage[pb][:, g * 4:(g + 1) * 4, :].rearrange("p a b -> p (a b)"),
                                    ptf[:, g % 2, :]),
                                  waits=[(pe, pe.n)], inc=(ve, 1))
                                ptf_ve[g] = ve.n
                            E("sync", lambda e, h=h, pb=pb, half=half: e.dma_start(
                                out=ffn_scr[half * HT:(half + 1) * HT, h * 128:(h + 1) * 128]
                                .rearrange("(a p) b -> p a b", p=128),
                                in_=ffstage[pb][:]),
                              waits=[(ve, ve.n)], inc=(s_fst[pb], 16))
                            stv[pb] = s_fst[pb].n
                        f3pend[0] = f3tail
                        spare_drain(2)
                    if f3pend[0] is not None:
                        f3pend[0]()
                        f3pend[0] = None

            do_half(0)
            do_half(1)
            spare_drain(len(spare_ops))
            return (s_fst[0].n, s_fst[1].n)

    FFN_ST = phase_f()
    phase_fence([(s_fst[0], s_fst[0].n), (s_fst[1], s_fst[1].n)])

    def emit_wquant(w_ap, ocols, dst, stack, tag, nelem, precols=None):
        oc = min(128, ocols)
        nchunks = (ocols + oc - 1) // oc
        wbuf = [sb(f"{tag}_wb{i}", [128, NDT, oc], F32, stack) for i in range(2)]
        acc = sb(f"{tag}_acc", [128, 2], F32, stack)
        swb = sb(f"{tag}_swb", [128, 2], F32, stack)
        tm = sb(f"{tag}_tm", [128, NDT, oc], F32, stack)
        ti = sb(f"{tag}_ti", [128, NDT, oc], I32, stack)
        tv = sb(f"{tag}_tv", [128, NDT, oc], F32, stack)
        pss = ps(f"{tag}_ps", [128, 1], F32, stack)
        if precols is not None:
            col0, ncp = precols
            E("vector", lambda e: e.tensor_reduce(acc[:, 0:1], pre_red[:, col0:col0 + ncp],
                                                  AX.X, ALU.add),
              waits=[(ve, ve.n)], inc=(ve, 1))
        else:
            red = sb(f"{tag}_red", [128, max(nchunks, 1)], F32, stack)
            rdv = [ve.n, ve.n]
            for c in range(nchunks):
                bi = c % 2
                E("sync", lambda e, c=c, bi=bi: e.dma_start(
                    out=wbuf[bi][:], in_=w_ap[:, c * oc:(c + 1) * oc]
                    .rearrange("(t p) f -> p t f", p=128)),
                  waits=[(ve, rdv[bi]), (s_const, CONST_DONE)], inc=(s_wb[bi], 16))
                E("vector", lambda e, bi=bi, c=c: e.tensor_reduce(
                    red[:, c:c + 1], wbuf[bi][:].rearrange("p a b -> p (a b)"), AX.X, ALU.add,
                    apply_absolute_value=True),
                  waits=[(s_wb[bi], s_wb[bi].n), (ve, ve.n)], inc=(ve, 1))
                rdv[bi] = ve.n
            E("vector", lambda e: e.tensor_reduce(acc[:, 0:1], red[:], AX.X, ALU.add),
              waits=[(ve, ve.n)], inc=(ve, 1))
        E("tensor", lambda e: e.matmul(pss[0:1, 0:1], acc[:, 0:1], onesf, start=True, stop=True),
          waits=[(ve, ve.n)], inc=(pe, 1))
        E("scalar", lambda e: e.activation(acc[0:1, 0:1], pss[0:1, 0:1], AF.Identity,
                                           scale=1.0 / nelem, bias=epsb[0:1, :]),
          waits=[(pe, pe.n), (ac, ac.n)], inc=(ac, 1))
        E("vector", lambda e: e.reciprocal(acc[0:1, 1:2], acc[0:1, 0:1]),
          waits=[(ac, ac.n), (ve, ve.n)], inc=(ve, 1))
        E("tensor", lambda e: e.matmul(pss[:, 0:1], onesrow,
                                       acc[0:1, 1:2], start=True, stop=True),
          waits=[(ve, ve.n)], inc=(pe, 1))
        E("vector", lambda e: e.tensor_copy(swb[:, 0:1], pss[:, 0:1]),
          waits=[(pe, pe.n), (ve, ve.n)], inc=(ve, 1))
        E("tensor", lambda e: e.matmul(pss[:, 0:1], onesrow,
                                       acc[0:1, 0:1], start=True, stop=True),
          waits=[(ve, ve.n)], inc=(pe, 1))
        E("vector", lambda e: e.tensor_copy(swb[:, 1:2], pss[:, 0:1]),
          waits=[(pe, pe.n), (ve, ve.n)], inc=(ve, 1))
        SWOK = (ve, ve.n)
        qac = [0, 0]
        ti_ve = [ve.n]
        for c in range(nchunks):
            bi = c % 2
            E("sync", lambda e, c=c, bi=bi: e.dma_start(
                out=wbuf[bi][:], in_=w_ap[:, c * oc:(c + 1) * oc]
                .rearrange("(t p) f -> p t f", p=128)),
              waits=[SWOK, (ac, qac[bi])], inc=(s_wb[bi], 16))
            E("scalar", lambda e, bi=bi: e.activation(
                tm[:].rearrange("p a b -> p (a b)"), wbuf[bi][:].rearrange("p a b -> p (a b)"),
                AF.Identity, scale=swb[:, 0:1], bias=bC1),
              waits=[(s_wb[bi], s_wb[bi].n), (ac, ac.n)], inc=(ac, 1))
            E("scalar", lambda e: e.activation(
                ti[:].rearrange("p a b -> p (a b)"), tm[:].rearrange("p a b -> p (a b)"),
                AF.Identity, bias=bmC1),
              waits=[(ac, ac.n), (ve, ti_ve[0])], inc=(ac, 1))
            qac[bi] = ac.n
            E("vector", lambda e: e.tensor_scalar(
                tv[:].rearrange("p a b -> p (a b)"), ti[:].rearrange("p a b -> p (a b)"),
                -1.0, 1.0, ALU.max, ALU.min),
              waits=[(ac, ac.n), (ve, ve.n)], inc=(ve, 1))
            ti_ve[0] = ve.n
            E("vector", lambda e, c=c: e.tensor_scalar(
                dst[:, :, c * oc:(c + 1) * oc],
                tv[:], swb[:, 1:2], None, ALU.mult),
              waits=[(ve, ve.n)], inc=(ve, 1))
        return (ve, ve.n)

    kv_stack = ExitStack()
    kT = sb("kT", [128, NKV, S], BF16, kv_stack, side="right")
    vT = sb("vT", [128, S // 128, NKV * DH], BF16, kv_stack, side="right")

    def phase_b1():
        a1 = ExitStack()
        with a1:
            ktern = sb("ktern", [128, NDT, NKV * DH], BF16, a1)
            vtern = sb("vtern", [128, NDT, NKV * DH], BF16, a1)
            wsub1 = ExitStack()
            with wsub1:
                emit_wquant(wk_t, NKV * DH, ktern, wsub1, "wk", NKV * DH * D, precols=(0, 16))
                emit_wquant(wv_t, NKV * DH, vtern, wsub1, "wv", NKV * DH * D, precols=(16, 16))
            b1 = ExitStack()
            with b1:
                NCH = 16
                ccks = sb("ccks", [128, 2, 256], F32, b1)
                ssks = sb("ssks", [128, 2, 256], F32, b1)
                xch = sb("b1_xch", [128, 3, 2, D], F32, b1)
                xn_t = sb("b1_xn", [128, D], F32, b1)
                xi_t = sb("b1_xi", [128, 2, D], I32, b1)
                xq_t = sb("b1_xq", [128, 2, D], BF16, b1)
                sxv = sb("b1_sx", [128, 2], F32, b1)
                mxv = sb("b1_mx", [128, 2], F32, b1)
                xqT = sb("b1_xqT", [128, 1, NDT, 256], BF16, b1)
                krot = sb("b1_krot", [128, 256], F32, b1)
                k1b = sb("b1_k1", [128, 256], F32, b1)
                stats = ln_stats(b1, "b1", c_junk, nt=2)
                ptr = ps("b1_ptr", [128, 2, 1024], BF16, b1)
                pkv = [ps(f"b1_pkv{i}", [128, 512], F32, b1) for i in range(2)]
                xqT_pe = [pe.n, pe.n]
                xch_done = [(ve.n, ac.n)] * 3
                rope_done = [ve.n, ve.n]
                pkv_ve = [ve.n, ve.n]
                qst = dict(tr_pe=[pe.n, pe.n], xi_ve=[ve.n, ve.n], xi=xi_t, nxi=2)
                xn_ve = [ve.n, ve.n]
                ptr_ve = [ve.n, ve.n]
                pend = [None]
                for c in range(NCH):
                    cb = c % 2
                    cb3 = c % 3
                    ldw = (s_xb[cb3], s_xb[cb3].n + 16)
                    E("sync", lambda e, c=c, cb3=cb3: e.dma_start(
                        out=xch[:, cb3, :, :], in_=x_kv[c * 256:(c + 1) * 256, :]
                        .rearrange("(t p) f -> p t f", p=128)),
                      waits=[(ve, xch_done[cb3][0]), (ac, xch_done[cb3][1])], inc=(s_xb[cb3], 16))
                    E("gpsimd", lambda e, c=c, cb=cb: e.dma_start(
                        out=ccks[:, cb, :], in_=cck[:, c * 256:(c + 1) * 256]),
                      waits=[(ve, rope_done[cb])], inc=(s_rope[cb], 16))
                    E("gpsimd", lambda e, c=c, cb=cb: e.dma_start(
                        out=ssks[:, cb, :], in_=ssk[:, c * 256:(c + 1) * 256]),
                      waits=[(ve, rope_done[cb])], inc=(s_rope[cb], 16))
                    RTOK = (s_rope[cb], s_rope[cb].n)

                    E("vector", None, waits=[(pe, xqT_pe[0])])
                    stat_done = emit_ln(xch[:, cb3, :, :], stats, ldw)
                    round_acs = []
                    for t in range(2):
                        tp = t % 2
                        E("scalar", lambda e, t=t, cb3=cb3, tp=tp: e.activation(
                            xn_t[:], xch[:, cb3, t, :], AF.Identity,
                            bias=stats["nb"][:, t:t + 1], scale=stats["rstd"][:, t:t + 1]),
                          waits=[stat_done, (ve, xn_ve[0]), (ac, ac.n)], inc=(ac, 1))
                        nac = ac.n
                        round_acs.append(emit_quant_a(xn_t[:], mxv, sxv,
                                                      slice(t, t + 1), (ac, nac), qst))
                        xn_ve[0] = ve.n
                    if pend[0] is not None:
                        pend[0]()
                        pend[0] = None
                    for t in range(2):
                        tp = t % 2
                        emit_quant_b(xq_t[:, tp, :], mxv, slice(t, t + 1), round_acs[t], qst)
                        rdy = (ve, ve.n)
                        emit_tr16(xq_t[:, tp, :],
                                  lambda g, t=t: xqT[:, 0, g * 8:(g + 1) * 8,
                                                     t * 128:(t + 1) * 128],
                                  ptr, rdy, t, ptr_ve, extra_w=[(pe, xqT_pe[0])])
                        qst["tr_pe"][tp] = pe.n
                    xch_done[cb3] = (ve.n, ac.n)
                    XQT = (ve, ve.n)

                    def mms_b1(c=c, cb=cb, XQT=XQT, RTOK=RTOK):
                     for kv in range(NKV):
                        pb = kv % 2
                        E("tensor", lambda e, kv=kv, pb=pb: [
                            e.matmul(pkv[pb][:, 0:256], ktern[:, dt, kv * 128:(kv + 1) * 128],
                                     xqT[:, 0, dt, :], start=(dt == 0), stop=(dt == NDT - 1))
                            for dt in range(NDT)][-1],
                          waits=[XQT, (ve, pkv_ve[pb])], inc=(pe, 1))
                        E("vector", lambda e, pb=pb, cb=cb: e.tensor_tensor(
                            k1b[:], pkv[pb][:, 0:256], ccks[:, cb, :], ALU.mult),
                          waits=[(pe, pe.n), RTOK, (ve, ve.n)], inc=(ve, 1))
                        E("vector", lambda e, pb=pb: e.tensor_copy(krot[0:64, :], pkv[pb][64:128, 0:256]),
                          waits=[(ve, ve.n)], inc=(ve, 1))
                        E("vector", lambda e, pb=pb: e.tensor_copy(krot[64:128, :], pkv[pb][0:64, 0:256]),
                          waits=[(ve, ve.n)], inc=(ve, 1))
                        pkv_ve[pb] = ve.n
                        E("vector", lambda e, cb=cb: e.tensor_tensor(
                            krot[:], krot[:], ssks[:, cb, :], ALU.mult),
                          waits=[(ve, ve.n)], inc=(ve, 1))
                        E("vector", lambda e, kv=kv, c=c: e.tensor_add(
                            kT[:, kv, c * 256:(c + 1) * 256], k1b[:], krot[:]),
                          waits=[(ve, ve.n)], inc=(ve, 1))
                     rope_done[cb] = ve.n
                     for t in range(2):
                        pb = t % 2
                        E("tensor", lambda e, t=t, pb=pb: [
                            e.matmul(pkv[pb][:], xqT[:, 0, dt, t * 128:(t + 1) * 128],
                                     vtern[:, dt, :], start=(dt == 0), stop=(dt == NDT - 1))
                            for dt in range(NDT)][-1],
                          waits=[XQT, (ve, pkv_ve[pb])], inc=(pe, 1))
                        E("vector", lambda e, t=t, c=c, pb=pb: e.tensor_copy(
                            vT[:, c * 2 + t, :], pkv[pb][:]),
                          waits=[(pe, pe.n), (ve, ve.n)], inc=(ve, 1))
                        pkv_ve[pb] = ve.n
                     xqT_pe[0] = pe.n
                    pend[0] = mms_b1
                if pend[0] is not None:
                    pend[0]()
                    pend[0] = None

    phase_b1()
    phase_fence()

    def phase_b2():
        a2 = ExitStack()
        with a2:
            qtern = sb("qtern", [128, NDT, D], BF16, a2)
            wsub = ExitStack()
            with wsub:
                emit_wquant(wq_t, D, qtern, wsub, "wq", D * D, precols=(32, 64))
            b2 = ExitStack()
            with b2:
                ccqs = sb("ccqs", [128, 2, 256], F32, b2)
                ssqs = sb("ssqs", [128, 2, 256], F32, b2)
                xch = sb("b2_xch", [128, 1, 2, D], F32, b2)
                xn_t = sb("b2_xn", [128, D], F32, b2)
                xi_t = sb("b2_xi", [128, 2, D], I32, b2)
                xq_t = sb("b2_xq", [128, 2, D], BF16, b2)
                sxv = sb("b2_sx", [128, 2], F32, b2)
                mxv = sb("b2_mx", [128, 2], F32, b2)
                xqT = sb("b2_xqT", [128, 1, NDT, 256], BF16, b2)
                qrot = sb("b2_qrot", [128, 256], F32, b2)
                q1b = sb("b2_q1", [128, 256], F32, b2)
                qbf = [sb(f"b2_qbf{i}", [128, 256], BF16, b2) for i in range(2)]
                stats = ln_stats(b2, "b2", c_junk, nt=2)
                ptr = ps("b2_ptr", [128, 2, 1024], BF16, b2)
                pq = [ps(f"b2_pq{i}", [128, 256], F32, b2) for i in range(2)]
                xqT_pe = [pe.n, pe.n]
                pq_ve = [ve.n, ve.n]
                qbf_st = [0, 0]
                xch_done = [(ve.n, ac.n)] * 2
                rope_done = [ve.n, ve.n]
                qst = dict(tr_pe=[pe.n, pe.n], xi_ve=[ve.n, ve.n], xi=xi_t, nxi=2)
                xn_ve = [ve.n, ve.n]
                ptr_ve = [ve.n, ve.n]
                pend = [None]
                for c in range(8):
                    cb = c % 2
                    ldw = (s_x[cb], s_x[cb].n + 16)
                    E("sync", lambda e, c=c: e.dma_start(
                        out=xch[:, 0, :, :], in_=x_q[c * 256:(c + 1) * 256, :]
                        .rearrange("(t p) f -> p t f", p=128)),
                      waits=[(ve, xch_done[0][0]), (ac, xch_done[0][1])], inc=(s_x[cb], 16))
                    E("gpsimd", lambda e, c=c, cb=cb: e.dma_start(
                        out=ccqs[:, cb, :], in_=ccq[:, c * 256:(c + 1) * 256]),
                      waits=[(ve, rope_done[cb])], inc=(s_rope[cb], 16))
                    E("gpsimd", lambda e, c=c, cb=cb: e.dma_start(
                        out=ssqs[:, cb, :], in_=ssq[:, c * 256:(c + 1) * 256]),
                      waits=[(ve, rope_done[cb])], inc=(s_rope[cb], 16))
                    RTOK = (s_rope[cb], s_rope[cb].n)

                    E("vector", None, waits=[(pe, xqT_pe[0])])
                    stat_done = emit_ln(xch[:, 0, :, :], stats, ldw)
                    round_acs = []
                    for t in range(2):
                        tp = t % 2
                        E("scalar", lambda e, t=t, cb=cb, tp=tp: e.activation(
                            xn_t[:], xch[:, 0, t, :], AF.Identity,
                            bias=stats["nb"][:, t:t + 1], scale=stats["rstd"][:, t:t + 1]),
                          waits=[stat_done, (ve, xn_ve[0]), (ac, ac.n)], inc=(ac, 1))
                        nac = ac.n
                        round_acs.append(emit_quant_a(xn_t[:], mxv, sxv,
                                                      slice(t, t + 1), (ac, nac), qst))
                        xn_ve[0] = ve.n
                    if pend[0] is not None:
                        pend[0]()
                        pend[0] = None
                    for t in range(2):
                        tp = t % 2
                        emit_quant_b(xq_t[:, tp, :], mxv, slice(t, t + 1), round_acs[t], qst)
                        rdy = (ve, ve.n)
                        emit_tr16(xq_t[:, tp, :],
                                  lambda g, t=t: xqT[:, 0, g * 8:(g + 1) * 8,
                                                     t * 128:(t + 1) * 128],
                                  ptr, rdy, t, ptr_ve, extra_w=[(pe, xqT_pe[0])])
                        qst["tr_pe"][tp] = pe.n
                    xch_done[0] = (ve.n, ac.n)
                    XQT = (ve, ve.n)

                    def mms_b2(c=c, cb=cb, XQT=XQT, RTOK=RTOK):
                     for h in range(NH):
                        pb = h % 2
                        E("tensor", lambda e, h=h, pb=pb: [
                            e.matmul(pq[pb][:], qtern[:, dt, h * 128:(h + 1) * 128],
                                     xqT[:, 0, dt, :], start=(dt == 0), stop=(dt == NDT - 1))
                            for dt in range(NDT)][-1],
                          waits=[XQT, (ve, pq_ve[pb])], inc=(pe, 1))
                        E("vector", lambda e, pb=pb, cb=cb: e.tensor_tensor(
                            q1b[:], pq[pb][:], ccqs[:, cb, :], ALU.mult),
                          waits=[(pe, pe.n), RTOK, (ve, ve.n)], inc=(ve, 1))
                        E("vector", lambda e, pb=pb: e.tensor_copy(qrot[0:64, :], pq[pb][64:128, :]),
                          waits=[(ve, ve.n)], inc=(ve, 1))
                        E("vector", lambda e, pb=pb: e.tensor_copy(qrot[64:128, :], pq[pb][0:64, :]),
                          waits=[(ve, ve.n)], inc=(ve, 1))
                        pq_ve[pb] = ve.n
                        E("vector", lambda e, cb=cb: e.tensor_tensor(qrot[:], qrot[:], ssqs[:, cb, :], ALU.mult),
                          waits=[(ve, ve.n)], inc=(ve, 1))
                        E("vector", lambda e, pb=pb: e.tensor_add(qbf[pb][:], q1b[:], qrot[:]),
                          waits=[(ve, ve.n), (s_qst[pb], qbf_st[pb])], inc=(ve, 1))
                        E("sync", lambda e, h=h, c=c, pb=pb: e.dma_start(
                            out=q_scr[h, :, c * 256:(c + 1) * 256], in_=qbf[pb][:]),
                          waits=[(ve, ve.n)], inc=(s_qst[pb], 16))
                        qbf_st[pb] = s_qst[pb].n
                     xqT_pe[0] = pe.n
                     rope_done[cb] = ve.n
                    pend[0] = mms_b2
                if pend[0] is not None:
                    pend[0]()
                    pend[0] = None
            return_val = (s_qst[0].n, s_qst[1].n)
        return return_val

    QSCR_ST = phase_b2()
    phase_fence([(s_qst[0], s_qst[0].n), (s_qst[1], s_qst[1].n)])

    ctx_stack = ExitStack()
    ctxT = sb("ctxT", [128, NH, TOWN], BF16, ctx_stack)
    se_t = sb("se_t", [128, 16, NH], F32, ctx_stack)

    def phase_c():
        cstack = ExitStack()
        with cstack:
            cmask = sb("cmask", [128, 32, 512], BF16, cstack)
            cfence = [(ve, ve.n), (ac, ac.n), (pe, pe.n)]
            E("sync", lambda e: e.dma_start(
                out=cmask[:], in_=masks.rearrange("a p b -> p a b")), waits=cfence,
              inc=(s_mask, 16))
            MASKOK = (s_mask, s_mask.n)
            qh = [sb(f"c_qh{i}", [128, 512], BF16, cstack) for i in range(2)]
            sev = sb("c_sev", [16, 512], F32, cstack)
            pp = sb("c_pp", [128, 8, 512], BF16, cstack)
            psc = [ps(f"c_psc{i}", [128, 1024], F32, cstack) for i in range(2)]
            ppv = [ps(f"c_ppv{i}", [128, 512], F32, cstack) for i in range(2)]
            pse = ps("c_pse", [16, 512], F32, cstack)
            pstr = ps("c_pstr", [128, 16], F32, cstack)
            qh_pe = [0, 0]
            ppv_evac_ve = [0, 0]
            se_evac_ve = [0]
            exp_ac = []
            cons_pe = []
            scor_pe = []

            pairs = []
            for j in range(4):
                cap = SLOT_KV[j]
                for h in range(NH):
                    for m in range(cap // 2):
                        pairs.append(dict(
                            j=j, h=h, kvh=h // 4, hb=h % 2, k0=2 * m, k1=2 * m + 1,
                            cap=cap, fetch=(m == 0),
                            fse=(h == 0 and m == 0),
                            lse=(h == NH - 1 and m == cap // 2 - 1),
                            fpv=(m == 0), lpv=(m == cap // 2 - 1),
                            last_of_head=(m == cap // 2 - 1),
                            last_of_slot=(h == NH - 1 and m == cap // 2 - 1),
                        ))

            def emit_consume(gp):
                P = pairs[gp]
                h, kvh, j, cap = P["h"], P["kvh"], P["j"], P["cap"]
                k0, k1 = P["k0"], P["k1"]
                w_cons = [(ac, exp_ac[gp]), (ve, P["mask_ve"])]
                if P["fpv"]:
                    w_cons.append((ve, ppv_evac_ve[h % 2]))
                    if P["fse"]:
                        w_cons.append((ve, se_evac_ve[0]))
                E("tensor", lambda e, h=h, kvh=kvh, k0=k0, k1=k1,
                    fse=P["fse"], fpv=P["fpv"], lse=P["lse"], lpv=P["lpv"]: [
                    e.matmul(pse[:], c_onehot[:, 16 * h:16 * h + 16], pp[:, k0 % 8, :],
                             start=fse, stop=False),
                    e.matmul(pse[:], c_onehot[:, 16 * h:16 * h + 16], pp[:, k1 % 8, :],
                             start=False, stop=lse),
                    e.matmul(ppv[h % 2][:], vT[:, k0, kvh * 128:(kvh + 1) * 128],
                             pp[:, k0 % 8, :], start=fpv, stop=False),
                    e.matmul(ppv[h % 2][:], vT[:, k1, kvh * 128:(kvh + 1) * 128],
                             pp[:, k1 % 8, :], start=False, stop=lpv)][-1],
                  waits=w_cons, inc=(pe, 1))
                cons_pe.append(pe.n)
                assert len(cons_pe) == gp + 1
                if P["last_of_head"]:
                    E("vector", lambda e, j=j, h=h: e.tensor_copy(
                        ctxT[:, h, j * 512:(j + 1) * 512], ppv[h % 2][:]),
                      waits=[(pe, pe.n), (ve, ve.n)], inc=(ve, 1))
                    ppv_evac_ve[h % 2] = ve.n
                if P["last_of_slot"]:
                    E("vector", lambda e: e.tensor_copy(sev[:], pse[:]),
                      waits=[(pe, pe.n), (ve, ve.n)], inc=(ve, 1))
                    for qs in range(4):
                        E("tensor", lambda e, qs=qs: e.transpose(
                            pstr[:], sev[:, qs * 128:(qs + 1) * 128], c_identf[0:16, 0:16]),
                          waits=[(ve, ve.n)], inc=(pe, 1))
                        E("vector", lambda e, j=j, qs=qs: e.tensor_copy(
                            se_t[:, j * 4 + qs, :], pstr[:]),
                          waits=[(pe, pe.n), (ve, ve.n)], inc=(ve, 1))
                    se_evac_ve[0] = ve.n

            QOKS = {}
            for gp, P in enumerate(pairs):
                j, h, hb, kvh = P["j"], P["h"], P["hb"], P["kvh"]
                k0, k1, cap = P["k0"], P["k1"], P["cap"]
                if P["fetch"]:
                    qw = [(pe, qh_pe[hb])]
                    if j == 0 and h < 2:
                        qw += [(s_qst[0], QSCR_ST[0]), (s_qst[1], QSCR_ST[1])] + cfence
                    E("sync", lambda e, j=j, h=h, hb=hb: e.dma_start(
                        out=qh[hb][:], in_=q_scr[h, :, j * 512:(j + 1) * 512]), waits=qw,
                      inc=(s_qh[hb], 16))
                    QOKS[(j, h)] = (s_qh[hb], s_qh[hb].n)
                sbk = gp % 2
                w_sc = [QOKS[(j, h)]]
                if gp == 0:
                    w_sc = w_sc + cfence
                if gp >= 2:
                    w_sc.append((ac, exp_ac[gp - 2]))
                E("tensor", lambda e, h=h, kvh=kvh, k0=k0, k1=k1, sbk=sbk, hb=hb: [
                    e.matmul(psc[sbk][:, 0:512], kT[:, kvh, k0 * 128:(k0 + 1) * 128],
                             qh[hb][:], start=True, stop=True),
                    e.matmul(psc[sbk][:, 512:1024], kT[:, kvh, k1 * 128:(k1 + 1) * 128],
                             qh[hb][:], start=True, stop=True)][-1],
                  waits=w_sc, inc=(pe, 1))
                scor_pe.append(pe.n)
                if P["last_of_head"]:
                    qh_pe[hb] = pe.n
                w_exp = [(pe, pe.n)]
                if gp >= 4:
                    w_exp.append((pe, cons_pe[gp - 4]))
                E("scalar", lambda e, k0=k0, sbk=sbk: e.activation(
                    pp[:, (k0 % 8):(k0 % 8) + 2, :].rearrange("p a b -> p (a b)"),
                    psc[sbk][:], AF.Exp, scale=SCALE),
                  waits=w_exp, inc=(ac, 1))
                exp_ac.append(ac.n)
                mve = 0
                for k in (k0, k1):
                    if k >= cap - 8:
                        mi = j * 8 + (k - (cap - 8))
                        E("vector", lambda e, k=k, mi=mi: e.tensor_mul(
                            pp[:, k % 8, :], pp[:, k % 8, :], cmask[:, mi, :]),
                          waits=[(ac, ac.n), MASKOK, (ve, ve.n)], inc=(ve, 1))
                        mve = ve.n
                P["mask_ve"] = mve
                if gp >= 1:
                    emit_consume(gp - 1)
            emit_consume(len(pairs) - 1)


    phase_c()
    phase_fence()

    kv_stack.close()
    cq_stack = ExitStack()
    cqT = sb("cqT", [128, NDT, TOWN], BF16, cq_stack, side="right")
    rsxo = sb("rsxo", [128, 16], F32, cq_stack, side="right")
    def phase_d():
        dstack = ExitStack()
        with dstack:
            ctok = sb("d_ctok", [128, D], BF16, dstack)
            cn = sb("d_cn", [128, 2, D], F32, dstack)
            ci = sb("d_ci", [128, D], I32, dstack)
            cq = sb("d_cq", [128, D], BF16, dstack)
            mx = sb("d_mx", [128, 1], F32, dstack)
            sx = sb("d_sx", [128, 2], F32, dstack)
            rse = sb("d_rse", [128, 16, NH], F32, dstack)
            ptr = ps("d_ptr", [128, 2, 1024], BF16, dstack)
            ptr2 = ps("d_ptr2", [128, 2, 1024], BF16, dstack)
            E("vector", lambda e: e.reciprocal(rse[:].rearrange("p a b -> p (a b)"),
                                               se_t[:].rearrange("p a b -> p (a b)")),
              waits=[(ve, ve.n)], inc=(ve, 1))
            RSEOK = (ve, ve.n)
            aptr_ve = [ve.n, ve.n]
            cptr_ve = [ve.n, ve.n]
            cn_ac = [ac.n, ac.n]
            cq_ve = [ve.n]
            p2_ac = [ac.n] * 16

            def stage_a(tt):
                tp = tt % 2
                for g in range(2):
                    bank = ptr[:, g, :]
                    E("tensor", lambda e, g=g, tt=tt, bank=bank: [
                        e.transpose(bank[:, j * 128:(j + 1) * 128],
                                    ctxT[:, g * 8 + j, tt * 128:(tt + 1) * 128], c_identbf[:])
                        for j in range(8)][-1],
                      waits=[(ve, aptr_ve[g])], inc=(pe, 1))
                    E("vector", lambda e, g=g, bank=bank: e.tensor_copy(
                        ctok[:, g * 1024:(g + 1) * 1024], bank[:]),
                      waits=[(pe, pe.n), (ve, ve.n)], inc=(ve, 1))
                    aptr_ve[g] = ve.n

                def bc_mul(e, tt=tt, tp=tp):
                    a = ctok[:].rearrange("p (h d) -> p h d", d=128)
                    b = rse[:, tt, :].rearrange("p (h o) -> p h o", o=1)
                    a2, b2 = broadcast_tensor_aps(a, b)
                    return e.tensor_tensor(cn[:, tp, :].rearrange("p (h d) -> p h d", d=128),
                                           a2, b2, ALU.mult)
                E("vector", bc_mul, waits=[(ve, ve.n), RSEOK, (ac, cn_ac[tp])], inc=(ve, 1))

            def stage_b(tt):
                tp = tt % 2
                E("vector", lambda e, tp=tp: e.tensor_reduce(mx[:], cn[:, tp, :], AX.X, ALU.max,
                                                             apply_absolute_value=True),
                  waits=[(ve, ve.n)], inc=(ve, 1))
                E("vector", lambda e, tt=tt: e.tensor_scalar(
                    rsxo[:, tt:tt + 1], mx[:], 1e-5, 1.0 / 127.0, ALU.max, ALU.mult),
                  waits=[(ve, ve.n)], inc=(ve, 1))
                E("vector", lambda e, tt=tt, tp=tp: e.reciprocal(sx[:, tp:tp + 1],
                                                                 rsxo[:, tt:tt + 1]),
                  waits=[(ve, ve.n)], inc=(ve, 1))

            def stage_c(tt):
                tp = tt % 2
                E("scalar", lambda e, tp=tp: e.activation(cn[:, tp, :], cn[:, tp, :], AF.Identity,
                                                          scale=sx[:, tp:tp + 1], bias=bC1),
                  waits=[(ve, ve.n), (ac, ac.n)], inc=(ac, 1))
                E("scalar", lambda e, tp=tp: e.activation(ci[:], cn[:, tp, :], AF.Identity,
                                                          bias=bmC1),
                  waits=[(ac, ac.n), (ve, cq_ve[0])], inc=(ac, 1))
                cn_ac[tp] = ac.n
                p2_ac[tt] = ac.n

            def stage_d(tt):
                E("vector", lambda e: e.tensor_scalar(cq[:], ci[:], 1.0, None, ALU.mult),
                  waits=[(ac, p2_ac[tt]), (pe, pipe_pe[0])], inc=(ve, 1))
                cq_ve[0] = ve.n
                rdy = (ve, ve.n)
                emit_tr16(cq,
                          lambda g, tt=tt: cqT[:, g * 8:(g + 1) * 8, tt * 128:(tt + 1) * 128],
                          ptr2, rdy, tt, cptr_ve)
                pipe_pe[0] = pe.n

            pipe_pe = [pe.n]
            stage_a(0)
            stage_b(0)
            for tt in range(16):
                stage_c(tt)
                if tt + 1 < 16:
                    stage_a(tt + 1)
                    stage_b(tt + 1)
                stage_d(tt)

    phase_d()
    phase_fence()

    ctx_stack.close()
    def phase_e():
        estack = ExitStack()
        with estack:
            otern = sb("otern", [128, NDT, D], BF16, estack)
            wsub = ExitStack()
            with wsub:
                OTOK = emit_wquant(wo_t, D, otern, wsub, "wo", D * D, precols=(96, 64))
            xrow = [sb(f"e_x{i}", [128, D], F32, estack) for i in range(2)]
            frow = [sb(f"e_f{i}", [128, D], F32, estack) for i in range(2)]
            orow = [sb(f"e_o{i}", [128, D], F32, estack) for i in range(2)]
            po = [ps(f"e_po{i}", [128, 512], F32, estack) for i in range(2)]
            row_ve = [0, 0]
            po_ac = [0, 0]
            ost_oc = [[0] * 4, [0] * 4]
            efence = [(ve, ve.n), (ac, ac.n), (pe, pe.n)]
            for tt in range(16):
                tb = tt % 2
                E("sync", lambda e, tt=tt, tb=tb: e.dma_start(
                    out=xrow[tb][:], in_=x_q[tt * 128:(tt + 1) * 128, :]),
                  waits=[(ve, row_ve[tb])] + (efence if tt < 2 else []), inc=(s_xr[tb], 16))
                XOK = (s_xr[tb], s_xr[tb].n)
                fw = [(ve, row_ve[tb])]
                if tt < 2:
                    fw += [(s_fst[0], FFN_ST[0]), (s_fst[1], FFN_ST[1])] + efence
                E("gpsimd", lambda e, tt=tt, tb=tb: e.dma_start(
                    out=frow[tb][:], in_=ffn_scr[tt * 128:(tt + 1) * 128, :]),
                  waits=fw, inc=(s_fr[tb], 16))
                FROK = (s_fr[tb], s_fr[tb].n)
                for oc in range(4):
                    pb = oc % 2
                    sl = slice(oc * 512, (oc + 1) * 512)
                    E("tensor", lambda e, tt=tt, oc=oc, pb=pb: [
                        e.matmul(po[pb][:], cqT[:, dt, tt * 128:(tt + 1) * 128],
                                 otern[:, dt, oc * 512:(oc + 1) * 512],
                                 start=(dt == 0), stop=(dt == NDT - 1))
                        for dt in range(NDT)][-1],
                      waits=[(ac, po_ac[pb]), OTOK], inc=(pe, 1))
                    E("scalar", lambda e, tt=tt, pb=pb, tb=tb, sl=sl: e.activation(
                        orow[tb][:, sl], po[pb][:],
                        AF.Identity, scale=rsxo[:, tt:tt + 1]),
                      waits=[(pe, pe.n), (ac, ac.n), (s_ost[tb][oc], ost_oc[tb][oc])], inc=(ac, 1))
                    po_ac[pb] = ac.n
                    E("vector", lambda e, tb=tb, sl=sl: e.tensor_add(
                        orow[tb][:, sl], orow[tb][:, sl], xrow[tb][:, sl]),
                      waits=[(ac, ac.n), XOK, (ve, ve.n)], inc=(ve, 1))
                    E("vector", lambda e, tb=tb, sl=sl: e.tensor_add(
                        orow[tb][:, sl], orow[tb][:, sl], frow[tb][:, sl]),
                      waits=[(ve, ve.n), FROK], inc=(ve, 1))
                    E("sync", lambda e, tt=tt, tb=tb, sl=sl: e.dma_start(
                        out=out[tt * 128:(tt + 1) * 128, sl], in_=orow[tb][:, sl]),
                      waits=[(ve, ve.n)], inc=(s_ost[tb][oc], 16))
                    ost_oc[tb][oc] = s_ost[tb][oc].n
                row_ve[tb] = ve.n
            E("sync", None, waits=[(s_ost[i][j], s_ost[i][j].n)
                                   for i in range(2) for j in range(4)])

    phase_e()

    cq_stack.close()

    with nc.Block() as block:
        def runner(name):
            def _run(eng):
                for waits, fn, inc in prog.ops[name]:
                    for (s, v) in waits:
                        eng.wait_ge(s.h, v)
                    if fn is None:
                        continue
                    ins = fn(eng)
                    if inc is not None:
                        ins.then_inc(inc[0].h, inc[1])
            return _run

        block.sync(runner("sync"))
        block.tensor(runner("tensor"))
        block.vector(runner("vector"))
        block.scalar(runner("scalar"))
        block.gpsimd(runner("gpsimd"))

    root.close()
    return nc


_NC_CACHE = {}


def _get_nc():
    if "nc" not in _NC_CACHE:
        _NC_CACHE["nc"] = build_program()
    return _NC_CACHE["nc"]


def _host_constants():
    c = {}
    c["ident_bf"] = np.eye(128).astype(ml_dtypes.bfloat16)
    c["ident_f"] = np.eye(128, dtype=np.float32)
    oh = np.zeros((128, NH * 16), np.float32)
    for h in range(NH):
        oh[:, 16 * h + h] = 1.0
    c["onehot"] = oh.astype(ml_dtypes.bfloat16)
    consts = np.zeros((128, 132), np.float32)
    consts[:, 0] = 12582912.0
    consts[:, 1] = 1.0
    consts[:, 2] = 1e-5
    consts[:, 3] = -12582912.0
    consts[:, 4:132] = 1.0
    c["consts"] = consts
    freqs = (1.0 / (10000.0 ** (np.arange(0, DH, 2, dtype=np.float32) / DH))).astype(np.float32)
    t = np.arange(S, dtype=np.float32)
    ang = np.outer(t, freqs).astype(np.float32)
    c["_cos"] = np.cos(ang).astype(np.float32)
    c["_sin"] = np.sin(ang).astype(np.float32)
    return c


def _masks_for_parity(p):
    m = np.zeros((32, 128, 512), np.float32)
    for j in range(4):
        g = P_CH[p][j]
        cap = SLOT_KV[j]
        for idx, k in enumerate(range(cap - 8, cap)):
            kp = 128 * k + np.arange(128)[:, None]
            qi = 512 * g + np.arange(512)[None, :]
            m[j * 8 + idx] = (qi >= kp).astype(np.float32)
    return np.ascontiguousarray(m.astype(ml_dtypes.bfloat16))


def kernel(L0_attn_x, L0_attn_Wq, L0_attn_Wk, L0_attn_Wv, L0_attn_Wo,
           L0_ffn_gate_w, L0_ffn_up_w, L0_ffn_down_w,
           L0_ffn_gate_s, L0_ffn_up_s, L0_ffn_down_s):
    x = np.ascontiguousarray(np.asarray(L0_attn_x, dtype=np.float32))
    wq_t = np.ascontiguousarray(np.asarray(L0_attn_Wq, np.float32).T)
    wk_t = np.ascontiguousarray(np.asarray(L0_attn_Wk, np.float32).T)
    wv_t = np.ascontiguousarray(np.asarray(L0_attn_Wv, np.float32).T)
    wo_t = np.ascontiguousarray(np.asarray(L0_attn_Wo, np.float32).T)
    gw_t = np.ascontiguousarray(np.asarray(L0_ffn_gate_w, np.float32).T)
    uw_t = np.ascontiguousarray(np.asarray(L0_ffn_up_w, np.float32).T)
    dw_t = np.ascontiguousarray(np.asarray(L0_ffn_down_w, np.float32).T)
    gs = np.ascontiguousarray(np.asarray(L0_ffn_gate_s, np.float32).reshape(NIT, 128).T)
    us = np.ascontiguousarray(np.asarray(L0_ffn_up_s, np.float32).reshape(NIT, 128).T)
    ds = np.ascontiguousarray(np.asarray(L0_ffn_down_s, np.float32).reshape(NDT, 128).T)

    hc = _host_constants()
    cos, sin = hc["_cos"], hc["_sin"]
    cck = np.ascontiguousarray(np.concatenate([cos.T, cos.T], axis=0).astype(np.float32))
    ssk = np.ascontiguousarray(np.concatenate([-sin.T, sin.T], axis=0).astype(np.float32))

    in_maps = []
    for core in range(8):
        b, p = core // 2, core % 2
        xb = np.ascontiguousarray(x[b])
        rows = np.concatenate([np.arange(512 * g, 512 * (g + 1)) for g in P_CH[p]])
        xq_core = np.ascontiguousarray(xb[rows])
        cq_ = np.ascontiguousarray(np.concatenate([cos.T[:, rows], cos.T[:, rows]], axis=0))
        sq_ = np.ascontiguousarray(np.concatenate([-sin.T[:, rows], sin.T[:, rows]], axis=0))
        in_maps.append(dict(
            x_kv=xb, x_q=xq_core, wq_t=wq_t, wk_t=wk_t, wv_t=wv_t, wo_t=wo_t,
            gw_t=gw_t, uw_t=uw_t, dw_t=dw_t, gs=gs, us=us, ds=ds,
            masks=_masks_for_parity(p), ccq=cq_, ssq=sq_, cck=cck, ssk=ssk,
            ident_bf=hc["ident_bf"], ident_f=hc["ident_f"], onehot=hc["onehot"],
            consts=hc["consts"],
        ))

    nc = _get_nc()
    res = run_bass_kernel_spmd(nc, in_maps, core_ids=list(range(8)))

    outf = np.empty((B, S, D), np.float32)
    for core in range(8):
        b, p = core // 2, core % 2
        o = res.results[core]["out"]
        for j, g in enumerate(P_CH[p]):
            outf[b, 512 * g:512 * (g + 1)] = o[512 * j:512 * (j + 1)]
    return outf
